# version 03 (final)
# baseline (speedup 1.0000x reference)
import numpy as np
import ml_dtypes
import concourse.bass as bass
import concourse.mybir as mybir
from concourse.bass_utils import run_bass_kernel_spmd

F32 = mybir.dt.float32
BF16 = mybir.dt.bfloat16
I32 = mybir.dt.int32
U8 = mybir.dt.uint8
AF = mybir.ActivationFunctionType
ALU = mybir.AluOpType
AX = mybir.AxisListType

f32 = np.float32
DEBUG_OUTPUTS = []
D = 4096
II = 11008
NQ = 2048
NL = 3072
NGI = 86
EXP_SCALE = np.float32(1.0 / np.sqrt(128.0))
EXP_SHIFT = np.float32(-8.0)



def _sigma_perm(n):
    j = np.arange(n)
    r = j % 128
    src = np.where(r < 64, 2 * r, 2 * (r - 64) + 1)
    return (j // 128) * 128 + src


def _rope_tables(h):
    inv_freq = 1.0 / 10000.0 ** (np.arange(0, 128, 2, dtype=f32) / 128)
    pos = np.maximum(h * 2048 - 1024 + np.arange(NL), 0).astype(f32)
    fr = np.einsum('d,t->dt', inv_freq, pos)
    cos = np.concatenate([np.cos(fr), np.cos(fr)], 0).astype(f32)
    sin = np.concatenate([np.sin(fr), np.sin(fr)], 0).astype(f32)
    sin[:64] *= -1.0
    return np.ascontiguousarray(cos), np.ascontiguousarray(sin)


def _window_masks():
    r = np.arange(128)[:, None]
    c = np.arange(512)[None, :]
    m = []
    for j in range(4):
        m.append((r > c - 128 * j).astype(f32))
    for j in range(4):
        m.append((r + 128 * j <= c).astype(f32))
    return np.ascontiguousarray(np.concatenate(m, 1))


def _kvb(h):
    out = np.ones((128, NL), f32)
    if h == 0:
        out[:, :1024] = 0.0
    return out


def host_prep(inputs):
    Pi = _sigma_perm(II)
    shared = {
        "wqT": np.ascontiguousarray(inputs["L0_attn_Wq"].T),
        "wkT": np.ascontiguousarray(inputs["L0_attn_Wk"].T),
        "wvT": np.ascontiguousarray(inputs["L0_attn_Wv"].T),
        "woT": np.ascontiguousarray(inputs["L0_attn_Wo"].T),
        "gqw": np.ascontiguousarray(inputs["L0_ffn_gate_qweight"]),
        "uqw": np.ascontiguousarray(inputs["L0_ffn_up_qweight"]),
        "dqw": np.ascontiguousarray(inputs["L0_ffn_down_qweight"][Pi]),
        "gsc": np.ascontiguousarray(inputs["L0_ffn_gate_scales"][:, Pi]),
        "usc": np.ascontiguousarray(inputs["L0_ffn_up_scales"][:, Pi]),
        "dsc": np.ascontiguousarray(inputs["L0_ffn_down_scales"]),
        "gqz": np.ascontiguousarray(inputs["L0_ffn_gate_qzeros"]),
        "uqz": np.ascontiguousarray(inputs["L0_ffn_up_qzeros"]),
        "dqz": np.ascontiguousarray(inputs["L0_ffn_down_qzeros"]),
        "masks": _window_masks(),
        "ident": np.eye(128, dtype=f32),
    }
    x = inputs["L0_attn_x"]
    in_maps = []
    for c in range(8):
        b, h = c // 2, c % 2
        cos, sinsgn = _rope_tables(h)
        m = dict(shared)
        m["x_own"] = np.ascontiguousarray(x[b, h * 2048:(h + 1) * 2048])
        m["x_halo"] = (np.ascontiguousarray(x[b, h * 2048 - 1024:h * 2048])
                       if h == 1 else np.zeros((1024, D), f32))
        m["cos"] = cos
        m["sinsgn"] = sinsgn
        m["kvb"] = _kvb(h)
        in_maps.append(m)
    return in_maps



class Sem:
    def __init__(self, h):
        self.h = h
        self.n = 0

    def inc(self, instr, k=1):
        instr.then_inc(self.h, k)
        self.n += k
        return self.n

    def mark(self):
        return (self, self.n)


class Guard:
    def __init__(self, nslots):
        self.free = [[] for _ in range(nslots)]

    def wait(self, eng, i):
        for sem, val in self.free[i]:
            eng.wait_ge(sem.h, val)

    def set(self, i, *marks):
        self.free[i] = list(marks)


def _arena(nc, *specs):
    a = bass.AutoArena(nc)
    hs = [a.sbuf_tensor(n, list(sh), dt) for (n, sh, dt) in specs]
    a.allocate()
    return (a, *hs)


def build_nc():
    nc = bass.Bass("TRN2", target_bir_lowering=False, debug=False)

    ein = lambda name, shape, dt: nc.dram_tensor(name, list(shape), dt, kind="ExternalInput").ap()
    x_own = ein("x_own", (NQ, D), F32)
    x_halo = ein("x_halo", (1024, D), F32)
    wkT_d = ein("wkT", (D, D), F32)
    wqT_d = ein("wqT", (D, D), F32)
    wvT_d = ein("wvT", (D, D), F32)
    woT_d = ein("woT", (D, D), F32)
    gqw = ein("gqw", (D, II // 8), I32)
    uqw = ein("uqw", (D, II // 8), I32)
    dqw = ein("dqw", (II, D // 8), I32)
    gsc = ein("gsc", (32, II), F32)
    usc = ein("usc", (32, II), F32)
    dsc = ein("dsc", (NGI, D), F32)
    gqz = ein("gqz", (32, II // 8), I32)
    uqz = ein("uqz", (32, II // 8), I32)
    dqz = ein("dqz", (NGI, D // 8), I32)
    cosd = ein("cos", (128, NL), F32)
    sind = ein("sinsgn", (128, NL), F32)
    kvbd = ein("kvb", (128, NL), F32)
    masksd = ein("masks", (128, 4096), F32)
    identd = ein("ident", (128, 128), F32)
    out_d = nc.dram_tensor("out", [NQ, D], F32, kind="ExternalOutput").ap()

    scr = lambda name, shape, dt: nc.dram_tensor(
        name, list(shape), dt,
        kind="ExternalOutput" if name in DEBUG_OUTPUTS else "Internal").ap()
    xln_dbg = (scr("xln_dbg", (2, 128, 32 * 1536), BF16)
               if "xln_dbg" in DEBUG_OUTPUTS else None)
    stats_dbg = (scr("stats_dbg", (24, 128, 24), F32)
                 if "stats_dbg" in DEBUG_OUTPUTS else None)
    psd_dbg = (scr("psd_dbg", (4, 2, 4, 128, 512), F32)
               if "psd_dbg" in DEBUG_OUTPUTS else None)
    p4_dbg = (dict(hq=scr("hq_dbg", (4, 128, NGI * 512), BF16),
                   sx=scr("sx_dbg", (4, 32, 512), BF16),
                   shn=scr("shn_dbg", (4, NGI, 512), BF16),
                   x2=scr("x2ln_dbg", (4, 128, 32 * 512), BF16))
              if "p4_dbg" in DEBUG_OUTPUTS else None)
    q_hbm = scr("q_hbm", (D, NQ), BF16)
    k_hbm = scr("k_hbm", (D, NL), BF16)
    v_hbm = scr("v_hbm", (32, NL, 128), BF16)
    h1_hbm = scr("h1_hbm", (NQ, D), F32)
    wg_hbm = scr("wg_hbm", (NGI, 128, D), BF16)
    wu_hbm = scr("wu_hbm", (NGI, 128, D), BF16)
    wd_hbm = scr("wd_hbm", (16, 128, NGI, 256), BF16)
    szg_hbm = scr("szg_hbm", (32, II), BF16)
    szu_hbm = scr("szu_hbm", (32, II), BF16)
    szd_hbm = scr("szd_hbm", (NGI, D), BF16)

    ctxs = []

    def enter(cm):
        ctxs.append(cm)
        return cm.__enter__()

    pf = [enter(nc.psum_tensor(f"pf{i}", [128, 512], F32)) for i in range(6)]
    pbf = [enter(nc.psum_tensor(f"pbf{i}", [128, 1024], BF16)) for i in range(2)]
    pfg = Guard(6)
    pbg = Guard(2)

    def pe_flush(t, sem):
        return sem.inc(t.matmul(pf[5][0:1, 0:128], negones[:], ident_sb[:],
                                start=True, stop=True))

    def mksem(name):
        return Sem(enter(nc.semaphore(name)))

    ident_sb = enter(nc.sbuf_tensor("ident_sb", [128, 128], BF16))
    ones_bf = enter(nc.sbuf_tensor("ones_bf", [1, 128], BF16))
    negones = enter(nc.sbuf_tensor("negones", [128, 1], BF16))

    block = enter(nc.Block())

    boot = mksem("boot")

    @block.gpsimd
    def _(g):
        boot.inc(g.dma_start(out=ident_sb[:], in_=identd), 16)

    @block.vector
    def _(v):
        v.memset(ones_bf[:], 1.0)
        boot.inc(v.memset(negones[:], -1.0), 1)
    BOOT = boot.n

    p0_ld = mksem("p0_ld")
    p0_gld = mksem("p0_gld")
    p0_dv = mksem("p0_dv")
    p0_pe = mksem("p0_pe")
    p0_stA = mksem("p0_stA")
    p0_stB = mksem("p0_stB")

    _a0, zq_i32, z_u8, z_i32, s_f32, szt_bf = _arena(
        nc, ("zq_i32", (NGI, II // 8), I32), ("z_u8", (NGI, II), U8),
        ("z_i32", (NGI, D), I32),
        ("s_f32", (NGI, II), F32), ("szt_bf", (NGI, II), BF16))
    if True:
        for qz_d, s_d, n, rows, sz_out, sgn, nat in [
                (gqz, gsc, II, 32, szg_hbm, -1.0, False),
                (uqz, usc, II, 32, szu_hbm, -1.0, False),
                (dqz, dsc, D, NGI, szd_hbm, 1.0, True)]:
            st_before = p0_stA.n

            @block.sync
            def _(s, qz_d=qz_d, s_d=s_d, n=n, rows=rows, st_before=st_before):
                s.wait_ge(p0_stA.h, st_before)
                s.dma_start(out=zq_i32[0:rows, 0:n // 8], in_=qz_d).then_inc(p0_ld.h, 16)
                s.dma_start(out=s_f32[0:rows, 0:n], in_=s_d).then_inc(p0_ld.h, 16)
            p0_ld.n += 32
            ld_m = p0_ld.n

            @block.vector
            def _(v, n=n, rows=rows, ld_m=ld_m, sgn=sgn, nat=nat):
                v.wait_ge(p0_ld.h, ld_m)
                if nat:
                    last = None
                    for i in range(8):
                        last = v.tensor_scalar(
                            z_i32[0:rows, i:n:8], zq_i32[0:rows, 0:n // 8],
                            4 * i, 15, ALU.logical_shift_right, ALU.bitwise_and)
                    p0_dv.inc(last)
                    v.wait_ge(p0_dv.h, p0_dv.n)
                    p0_dv.inc(v.scalar_tensor_tensor(
                        szt_bf[0:rows, 0:n], z_i32[0:rows, 0:n], sgn,
                        s_f32[0:rows, 0:n], ALU.mult, ALU.mult))
                else:
                    u8v = zq_i32[0:rows, 0:n // 8].bitcast(U8).rearrange(
                        "p (b k) -> p b k", k=64)
                    z3 = z_u8[0:rows, 0:n].rearrange("p (b j) -> p b j", j=128)
                    v.tensor_scalar(z3[:, :, 0:64], u8v, 15, None, ALU.bitwise_and)
                    p0_dv.inc(v.tensor_scalar(z3[:, :, 64:128], u8v, 4, 15,
                                              ALU.logical_shift_right, ALU.bitwise_and))
                    v.wait_ge(p0_dv.h, p0_dv.n)
                    p0_dv.inc(v.scalar_tensor_tensor(
                        szt_bf[0:rows, 0:n], z_u8[0:rows, 0:n], sgn,
                        s_f32[0:rows, 0:n], ALU.mult, ALU.mult))
            dv_m = p0_dv.n

            @block.scalar
            def _(s, sz_out=sz_out, n=n, rows=rows, dv_m=dv_m):
                s.wait_ge(p0_dv.h, dv_m)
                p0_stA.inc(s.dma_start(out=sz_out, in_=szt_bf[0:rows, 0:n]), 16)
    _a0.free()
    SZ_DONE = p0_stA.n

    def tiles_of(n):
        o, out = 0, []
        while o < n:
            w = min(512, n - o)
            out.append((o, w))
            o += w
        return out

    strips = ([(gqw, g, II, gsc, wg_hbm, p0_stA, False) for g in range(32)]
              + [(uqw, g, II, usc, wu_hbm, p0_stB, False) for g in range(32)]
              + [(dqw, g, D, dsc, wd_hbm, p0_stA, True) for g in range(NGI)])

    _a0b, qw_t, q4_t, dq_i32, w_sb, sbc, srow = _arena(
        nc, ("qw_t", (128, 2 * (II // 8)), I32), ("q4_t", (128, 2 * II), U8),
        ("dq_i32", (128, 2 * D), I32),
        ("w_sb", (128, 2 * II), BF16), ("sbc", (128, 2 * 512), BF16),
        ("srow", (1, 2 * II), BF16))
    if True:

        m_unp, m_mul_last = {}, {}
        srow_g = Guard(2)
        wsb_g = Guard(2)
        P0_DV0 = p0_dv.n

        for i, (src, g, n, s_src, dst, stsem, nat) in enumerate(strips):
            buf = i % 2
            tls = tiles_of(n)

            @block.sync
            def _(s, i=i, src=src, g=g, n=n, buf=buf):
                if i == 0:
                    s.wait_ge(p0_dv.h, P0_DV0)
                    s.wait_ge(p0_stA.h, SZ_DONE)
                if i >= 2:
                    s.wait_ge(p0_dv.h, m_unp[i - 2])
                s.dma_start(out=qw_t[:, buf * (II // 8):buf * (II // 8) + n // 8],
                            in_=src[g * 128:(g + 1) * 128, :]).then_inc(p0_ld.h, 16)
            p0_ld.n += 16
            ld_mark = p0_ld.n

            @block.gpsimd
            def _(ge, i=i, g=g, n=n, s_src=s_src, buf=buf):
                ge.wait_ge(boot.h, BOOT)
                if i < 2:
                    ge.wait_ge(p0_dv.h, P0_DV0)
                    ge.wait_ge(p0_stA.h, SZ_DONE)
                srow_g.wait(ge, buf)
                ge.dma_start(out=srow[0:1, buf * II:buf * II + n],
                             in_=s_src[g:g + 1, :]).then_inc(p0_gld.h, 16)
            p0_gld.n += 16
            sld_mark = p0_gld.n

            @block.vector
            def _(v, i=i, n=n, buf=buf, ld_mark=ld_mark, nat=nat):
                v.wait_ge(p0_ld.h, ld_mark)
                wsb_g.wait(v, buf)
                if nat:
                    last = None
                    for k in range(8):
                        last = v.tensor_scalar(
                            dq_i32[:, buf * D + k:buf * D + n:8],
                            qw_t[:, buf * (II // 8):buf * (II // 8) + n // 8],
                            4 * k, 15, ALU.logical_shift_right, ALU.bitwise_and)
                    p0_dv.inc(last)
                else:
                    u8v = qw_t[:, buf * (II // 8):buf * (II // 8) + n // 8].bitcast(U8) \
                        .rearrange("p (b k) -> p b k", k=64)
                    q3 = q4_t[:, buf * II:buf * II + n].rearrange("p (b j) -> p b j", j=128)
                    v.tensor_scalar(q3[:, :, 0:64], u8v, 15, None, ALU.bitwise_and)
                    p0_dv.inc(v.tensor_scalar(q3[:, :, 64:128], u8v, 4, 15,
                                              ALU.logical_shift_right, ALU.bitwise_and))
            m_unp[i] = p0_dv.n

            for ti, (o, w) in enumerate(tls):
                bank = ti % 2

                @block.tensor
                def _(t, i=i, ti=ti, o=o, w=w, buf=buf, bank=bank, sld_mark=sld_mark):
                    if ti == 0:
                        t.wait_ge(p0_gld.h, sld_mark)
                    pfg.wait(t, bank)
                    t.matmul(pf[bank][:, 0:w], ones_bf[:],
                             srow[0:1, buf * II + o:buf * II + o + w],
                             start=True, stop=True)
                    pe_flush(t, p0_pe)
                pe_mark = p0_pe.n

                @block.vector
                def _(v, i=i, ti=ti, o=o, w=w, buf=buf, bank=bank, pe_mark=pe_mark,
                      nat=nat):
                    v.wait_ge(p0_pe.h, pe_mark)
                    if ti == 0:
                        v.wait_ge(p0_dv.h, m_unp[i])
                    p0_dv.inc(v.tensor_copy(sbc[:, bank * 512:bank * 512 + w],
                                            pf[bank][:, 0:w]))
                    v.wait_ge(p0_dv.h, p0_dv.n)
                    src_q = (dq_i32[:, buf * D + o:buf * D + o + w] if nat
                             else q4_t[:, buf * II + o:buf * II + o + w])
                    p0_dv.inc(v.tensor_tensor(
                        w_sb[:, buf * II + o:buf * II + o + w],
                        src_q, sbc[:, bank * 512:bank * 512 + w], ALU.mult))
                pfg.set(bank, p0_dv.mark())
            m_mul_last[i] = p0_dv.n
            srow_g.set(buf, p0_pe.mark())

            @block.scalar
            def _(s, i=i, g=g, n=n, buf=buf, dst=dst, stsem=stsem):
                s.wait_ge(p0_dv.h, m_mul_last[i])
                if dst is wd_hbm:
                    s.dma_start(
                        out=dst[:, :, g, :].rearrange("s p o -> p s o"),
                        in_=w_sb[:, buf * II:buf * II + n]
                            .rearrange("p (s o) -> p s o", o=256)) \
                        .then_inc(stsem.h, 16)
                    stsem.n += 16
                else:
                    bounds = [0, 22, 44, 65, 86]
                    for si in range(4):
                        i0, i1 = bounds[si], bounds[si + 1]
                        s.dma_start(
                            out=dst[i0:i1, :, g * 128:(g + 1) * 128]
                                .rearrange("i p c -> p i c"),
                            in_=w_sb[:, buf * II + i0 * 128:buf * II + i1 * 128]
                                .rearrange("p (i c) -> p i c", c=128)) \
                            .then_inc(stsem.h, 16)
                        stsem.n += 16
            wsb_g.set(buf, stsem.mark())

    _a0b.free()
    P0_STA = p0_stA.n
    P0_STB = p0_stB.n
    pfg.set(0, p0_dv.mark())
    pfg.set(1, p0_dv.mark())
    P0_ALL = [p0_dv.mark(), p0_pe.mark(), (p0_stA, P0_STA), (p0_stB, P0_STB)]

    p1_ld = mksem("p1_ld")
    p1_gld = mksem("p1_gld")
    p1_dv = mksem("p1_dv")
    p1_ac = mksem("p1_ac")
    p1_pe = mksem("p1_pe")
    p1_st = mksem("p1_st")
    p1_gp = mksem("p1_gp")

    _a1, cos_sb, sin_sb, x_ln, obf = _arena(
        nc, ("cos_sb", (128, NL), F32), ("sin_sb", (128, NL), F32),
        ("x_ln", (128, 32 * 1536), BF16), ("obf", (128, 2 * 512), BF16))
    if True:

        @block.sync
        def _(s):
            for sem, val in P0_ALL:
                s.wait_ge(sem.h, val)
            s.dma_start(out=cos_sb[:], in_=cosd).then_inc(p1_ld.h, 16)
            s.dma_start(out=sin_sb[:], in_=sind).then_inc(p1_ld.h, 16)
        p1_ld.n += 32
        TRIG = p1_ld.n

        xt_g = Guard(2)
        st_g = Guard(2)
        x2_g = Guard(2)
        wblk_g = Guard(2)
        vwblk_g = Guard(2)
        kraw_g = Guard(2)
        t2b_g = Guard(2)
        obf_g = Guard(2)
        lnd = {}
        ru = [0]
        wu = [0]
        vu = [0]

        def emit_ln_tile(half, tt, x_src, x_t, sqs, stats, x2tm, barrier):
            par = tt % 2
            sc0 = par * 12

            @block.sync
            def _(s):
                for mk in barrier:
                    s.wait_ge(mk[0].h, mk[1])
                xt_g.wait(s, par)
                s.dma_start(out=x_t[:, par * D:(par + 1) * D], in_=x_src).then_inc(p1_ld.h, 16)
            p1_ld.n += 16
            x_mark = p1_ld.n

            @block.scalar
            def _(sc):
                sc.wait_ge(p1_ld.h, x_mark)
                st_g.wait(sc, par)
                p1_ac.inc(sc.activation(sqs[:], x_t[:, par * D:(par + 1) * D],
                                        AF.Square, accum_out=stats[:, sc0 + 8:sc0 + 9]))
                sc.wait_ge(p1_ac.h, p1_ac.n)
                p1_ac.inc(sc.activation(stats[:, sc0:sc0 + 1], stats[:, sc0 + 8:sc0 + 9],
                                        AF.Copy, scale=np.float32(1.0 / D)))
            sq_mark = p1_ac.n

            @block.vector
            def _(v):
                v.wait_ge(p1_ld.h, x_mark)
                xx = x_t[:, par * D:(par + 1) * D]
                p1_dv.inc(v.reduce_sum(stats[:, sc0 + 9:sc0 + 10], xx, AX.X))
                v.wait_ge(p1_dv.h, p1_dv.n)
                p1_dv.inc(v.tensor_scalar(stats[:, sc0 + 1:sc0 + 2],
                                          stats[:, sc0 + 9:sc0 + 10],
                                          -1.0 / D, None, ALU.mult))
                v.wait_ge(p1_dv.h, p1_dv.n)
                v.wait_ge(p1_ac.h, sq_mark)
                p1_dv.inc(v.tensor_tensor(stats[:, sc0 + 2:sc0 + 3],
                                          stats[:, sc0 + 1:sc0 + 2],
                                          stats[:, sc0 + 1:sc0 + 2], ALU.mult))
                v.wait_ge(p1_dv.h, p1_dv.n)
                p1_dv.inc(v.scalar_tensor_tensor(
                    stats[:, sc0 + 2:sc0 + 3], stats[:, sc0:sc0 + 1], 1.0,
                    stats[:, sc0 + 2:sc0 + 3], ALU.mult, ALU.subtract))
            var_mark = p1_dv.n

            @block.scalar
            def _(sc):
                sc.wait_ge(p1_dv.h, var_mark)
                p1_ac.inc(sc.activation(stats[:, sc0 + 10:sc0 + 11],
                                        stats[:, sc0 + 2:sc0 + 3],
                                        AF.Ln, bias=np.float32(1e-5)))
                sc.wait_ge(p1_ac.h, p1_ac.n)
                p1_ac.inc(sc.activation(stats[:, sc0 + 3:sc0 + 4],
                                        stats[:, sc0 + 10:sc0 + 11],
                                        AF.Exp, scale=np.float32(-0.5)))
            rstd_mark = p1_ac.n

            @block.vector
            def _(v):
                v.wait_ge(p1_ac.h, rstd_mark)
                x2_g.wait(v, par)
                p1_dv.inc(v.tensor_scalar(
                    x2tm[:, par * D:(par + 1) * D], x_t[:, par * D:(par + 1) * D],
                    stats[:, sc0 + 1:sc0 + 2], stats[:, sc0 + 3:sc0 + 4],
                    ALU.add, ALU.mult))
            apply_mark = p1_dv.n
            if stats_dbg is not None:
                @block.sync
                def _(s):
                    s.wait_ge(p1_dv.h, apply_mark)
                    s.dma_start(out=stats_dbg[half * 12 + tt], in_=stats[:]) \
                        .then_inc(p1_st.h, 16)
                p1_st.n += 16
                st_g.set(par, (p1_dv, apply_mark), (p1_st, p1_st.n))
            else:
                st_g.set(par, (p1_dv, apply_mark))
            xt_g.set(par, (p1_dv, apply_mark))

            tp_last = None
            for grp in range(4):
                bank = grp % 2

                @block.tensor
                def _(t, grp=grp, bank=bank):
                    t.wait_ge(boot.h, BOOT)
                    t.wait_ge(p1_dv.h, apply_mark)
                    pbg.wait(t, bank)
                    for j in range(8):
                        fc = grp * 8 + j
                        t.transpose(pbf[bank][:, j * 128:(j + 1) * 128],
                                    x2tm[:, par * D + fc * 128:par * D + (fc + 1) * 128],
                                    ident_sb[:])
                    pe_flush(t, p1_pe)
                tp_mark = p1_pe.n
                tp_last = tp_mark

                @block.vector
                def _(v, grp=grp, bank=bank, tp_mark=tp_mark, tt=tt):
                    v.wait_ge(p1_pe.h, tp_mark)
                    dst = x_ln[:].rearrange("p (c t) -> p c t", t=1536)[
                        :, grp * 8:(grp + 1) * 8, tt * 128:(tt + 1) * 128]
                    p1_dv.inc(v.tensor_copy(
                        dst, pbf[bank][:].rearrange("p (c t) -> p c t", t=128)))
                pbg.set(bank, p1_dv.mark())
            x2_g.set(par, (p1_pe, tp_last))

        def emit_kq_unit(half, pname, oc, tfs, w_d, dst_hbm, wblk, kraw, krot, t1b, t2b, barrier):
            wpar = wu[0] % 2
            wu[0] += 1

            @block.gpsimd
            def _(ge):
                ge.wait_ge(boot.h, BOOT)
                for mk in barrier:
                    ge.wait_ge(mk[0].h, mk[1])
                wblk_g.wait(ge, wpar)
                for part in range(4):
                    ge.dma_start(
                        out=wblk[:, wpar * 4096:(wpar + 1) * 4096]
                            .rearrange("p (fcx o) -> p fcx o", o=128)[:, part * 8:(part + 1) * 8, :],
                        in_=w_d[part * 1024:(part + 1) * 1024, oc * 128:(oc + 1) * 128]
                            .rearrange("(fcx p) o -> p fcx o", p=128)) \
                        .then_inc(p1_gld.h, 16)
            p1_gld.n += 64
            wld_mark = p1_gld.n

            mm_last = None
            for tf in tfs:
                r = ru[0]
                ru[0] += 1
                rpar = r % 2
                bank = r % 4
                tcols = half * 1536 + tf * 512

                @block.tensor
                def _(t, tf=tf, bank=bank):
                    t.wait_ge(p1_gld.h, wld_mark)
                    t.wait_ge(p1_dv.h, lnd[half])
                    pfg.wait(t, bank)
                    for fc in range(32):
                        t.matmul(pf[bank][:],
                                 wblk[:, wpar * 4096 + fc * 128:wpar * 4096 + (fc + 1) * 128],
                                 x_ln[:, fc * 1536 + tf * 512:fc * 1536 + tf * 512 + 512],
                                 start=(fc == 0), stop=(fc == 31))
                    pe_flush(t, p1_pe)
                mm_mark = p1_pe.n
                mm_last = mm_mark

                @block.scalar
                def _(sc, bank=bank, rpar=rpar, mm_mark=mm_mark):
                    sc.wait_ge(p1_pe.h, mm_mark)
                    kraw_g.wait(sc, rpar)
                    sc.activation(kraw[:, rpar * 512:(rpar + 1) * 512],
                                  pf[bank][:], AF.Copy)
                    sc.activation(krot[0:64, rpar * 512:(rpar + 1) * 512],
                                  pf[bank][64:128, :], AF.Copy)
                    p1_ac.inc(sc.activation(krot[64:128, rpar * 512:(rpar + 1) * 512],
                                            pf[bank][0:64, :], AF.Copy))
                raw_mark = p1_ac.n
                pfg.set(bank, (p1_ac, raw_mark))

                @block.gpsimd
                def _(ge, rpar=rpar, tcols=tcols, raw_mark=raw_mark):
                    ge.wait_ge(p1_ac.h, raw_mark)
                    ge.wait_ge(p1_ld.h, TRIG)
                    t2b_g.wait(ge, rpar)
                    p1_gp.inc(ge.tensor_tensor(
                        t2b[:, rpar * 512:(rpar + 1) * 512],
                        krot[:, rpar * 512:(rpar + 1) * 512],
                        sin_sb[:, tcols:tcols + 512], ALU.mult))
                t2_mark = p1_gp.n

                @block.vector
                def _(v, rpar=rpar, tcols=tcols, raw_mark=raw_mark):
                    v.wait_ge(p1_ac.h, raw_mark)
                    v.wait_ge(p1_ld.h, TRIG)
                    p1_dv.inc(v.tensor_tensor(t1b[:, rpar * 512:(rpar + 1) * 512],
                                              kraw[:, rpar * 512:(rpar + 1) * 512],
                                              cos_sb[:, tcols:tcols + 512], ALU.mult))
                t1_mark = p1_dv.n
                kraw_g.set(rpar, (p1_dv, t1_mark), (p1_gp, t2_mark))

                @block.vector
                def _(v, rpar=rpar, t2_mark=t2_mark, t1_mark=t1_mark):
                    v.wait_ge(p1_gp.h, t2_mark)
                    v.wait_ge(p1_dv.h, t1_mark)
                    obf_g.wait(v, rpar)
                    p1_dv.inc(v.tensor_tensor(
                        obf[:, rpar * 512:(rpar + 1) * 512],
                        t1b[:, rpar * 512:(rpar + 1) * 512],
                        t2b[:, rpar * 512:(rpar + 1) * 512], ALU.add))
                add_mark = p1_dv.n
                t2b_g.set(rpar, (p1_dv, add_mark))

                @block.sync
                def _(s, oc=oc, tcols=tcols, rpar=rpar, add_mark=add_mark):
                    s.wait_ge(p1_dv.h, add_mark)
                    if pname == 'k':
                        dst = dst_hbm[oc * 128:(oc + 1) * 128, tcols:tcols + 512]
                    else:
                        qc = tcols - 1024
                        dst = dst_hbm[oc * 128:(oc + 1) * 128, qc:qc + 512]
                    s.dma_start(out=dst, in_=obf[:, rpar * 512:(rpar + 1) * 512]) \
                        .then_inc(p1_st.h, 16)
                p1_st.n += 16
                obf_g.set(rpar, (p1_st, p1_st.n))
            wblk_g.set(wpar, (p1_pe, mm_last))

        def emit_v_unit(half, oc, vwblk, barrier):
            vpar = vu[0] % 2
            vu[0] += 1

            @block.gpsimd
            def _(ge):
                for mk in barrier:
                    ge.wait_ge(mk[0].h, mk[1])
                vwblk_g.wait(ge, vpar)
                for part in range(4):
                    ge.dma_start(
                        out=vwblk[:, vpar * 16384:(vpar + 1) * 16384]
                            .rearrange("p (fcx o) -> p fcx o", o=512)[:, part * 8:(part + 1) * 8, :],
                        in_=wvT_d[part * 1024:(part + 1) * 1024, oc * 512:(oc + 1) * 512]
                            .rearrange("(fcx p) o -> p fcx o", p=128)) \
                        .then_inc(p1_gld.h, 16)
            p1_gld.n += 64
            vld_mark = p1_gld.n

            mm_last = None
            for tt in range(12):
                r = ru[0]
                ru[0] += 1
                rpar = r % 2
                bank = r % 4

                @block.tensor
                def _(t, tt=tt, bank=bank):
                    t.wait_ge(p1_gld.h, vld_mark)
                    t.wait_ge(p1_dv.h, lnd[half])
                    pfg.wait(t, bank)
                    for fc in range(32):
                        t.matmul(pf[bank][:],
                                 x_ln[:, fc * 1536 + tt * 128:fc * 1536 + (tt + 1) * 128],
                                 vwblk[:, vpar * 16384 + fc * 512:vpar * 16384 + (fc + 1) * 512],
                                 start=(fc == 0), stop=(fc == 31))
                    pe_flush(t, p1_pe)
                vmm_mark = p1_pe.n
                mm_last = vmm_mark

                @block.scalar
                def _(sc, bank=bank, rpar=rpar, vmm_mark=vmm_mark):
                    sc.wait_ge(p1_pe.h, vmm_mark)
                    obf_g.wait(sc, rpar)
                    p1_ac.inc(sc.activation(obf[:, rpar * 512:(rpar + 1) * 512],
                                            pf[bank][:], AF.Copy))
                vraw_mark = p1_ac.n
                pfg.set(bank, (p1_ac, vraw_mark))

                @block.sync
                def _(s, oc=oc, tt=tt, rpar=rpar, vraw_mark=vraw_mark):
                    s.wait_ge(p1_ac.h, vraw_mark)
                    trow = half * 1536 + tt * 128
                    for sub in range(4):
                        s.dma_start(
                            out=v_hbm[oc * 4 + sub, trow:trow + 128, :],
                            in_=obf[:, rpar * 512 + sub * 128:rpar * 512 + (sub + 1) * 128]) \
                            .then_inc(p1_st.h, 16)
                p1_st.n += 64
                obf_g.set(rpar, (p1_st, p1_st.n))
            vwblk_g.set(vpar, (p1_pe, mm_last))

        for half in range(2):
            if half == 1:
                gate = p1_pe.mark()
                for b in range(2):
                    pbg.set(b, *(pbg.free[b] + [gate]))
            barrier_a = [(p1_pe, p1_pe.n), (p1_dv, p1_dv.n), (p1_ac, p1_ac.n),
                         (p1_gp, p1_gp.n), (p1_st, p1_st.n)]
            _a1a, x_t, sqs, stats, x2tm = _arena(
                nc, (f"x_t{half}", (128, 2 * D), F32), (f"sqs{half}", (128, D), F32),
                (f"stats{half}", (128, 24), F32), (f"x2tm{half}", (128, 2 * D), BF16))
            for tt in range(12):
                lt = half * 12 + tt
                src = (x_halo[lt * 128:(lt + 1) * 128, :] if lt < 8
                       else x_own[(lt - 8) * 128:(lt - 8 + 1) * 128, :])
                emit_ln_tile(half, tt, src, x_t, sqs, stats, x2tm,
                             barrier_a if tt == 0 else [])
            lnd[half] = p1_dv.n
            if xln_dbg is not None:
                @block.sync
                def _(s, half=half):
                    s.wait_ge(p1_dv.h, lnd[half])
                    s.dma_start(out=xln_dbg[half], in_=x_ln[:]).then_inc(p1_st.h, 16)
                p1_st.n += 16
            _a1a.free()
            barrier_b = [(p1_pe, p1_pe.n), (p1_dv, p1_dv.n), (p1_ac, p1_ac.n)]
            _a1b, wblk, kraw, krot, t1b, t2b = _arena(
                nc, (f"wblk{half}", (128, 2 * 4096), BF16),
                (f"kraw{half}", (128, 2 * 512), F32),
                (f"krot{half}", (128, 2 * 512), F32),
                (f"t1b{half}", (128, 2 * 512), F32),
                (f"t2b{half}", (128, 2 * 512), F32))
            first = True
            for oc in range(32):
                emit_kq_unit(half, 'k', oc, [0, 1, 2], wkT_d, k_hbm,
                             wblk, kraw, krot, t1b, t2b, barrier_b if first else [])
                first = False
            qtfs = [2] if half == 0 else [0, 1, 2]
            for oc in range(32):
                emit_kq_unit(half, 'q', oc, qtfs, wqT_d, q_hbm,
                             wblk, kraw, krot, t1b, t2b, [])
            _a1b.free()
            barrier_c = [(p1_pe, p1_pe.n), (p1_dv, p1_dv.n), (p1_gp, p1_gp.n)]
            _a1c, vwblk = _arena(nc, (f"vwblk{half}", (128, 2 * 16384), BF16))
            for oc in range(8):
                emit_v_unit(half, oc, vwblk, barrier_c if oc == 0 else [])
            _a1c.free()

    _a1.free()
    P1_ST = p1_st.n
    P1_ALL = [(p1_pe, p1_pe.n), (p1_dv, p1_dv.n), (p1_ac, p1_ac.n),
              (p1_gp, p1_gp.n), (p1_st, P1_ST)]
    for b in range(4):
        pfg.set(b, (p1_ac, p1_ac.n))

    p2_ld = mksem("p2_ld")
    p2_gld = mksem("p2_gld")
    p2_dv = mksem("p2_dv")
    p2_ac = mksem("p2_ac")
    p2_pe = mksem("p2_pe")
    p3_ld = mksem("p3_ld")
    p3_gld = mksem("p3_gld")
    p3_dv = mksem("p3_dv")
    p3_pe = mksem("p3_pe")
    p3_st = mksem("p3_st")
    p3_ac2 = mksem("p3_ac2")

    _a23, attn_sb = _arena(nc, ("attn_sb", (128, 32 * 2048), BF16))
    if True:
        _a2, k_h, q_h, v_h, masks_bf, kvb_bf, expm, av_sb, den_sb, recip_sb = _arena(
            nc, ("k_h", (128, 2 * NL), BF16), ("q_h", (128, 2 * NQ), BF16),
            ("v_h", (128, 2 * NL), BF16), ("masks_bf", (128, 4096), BF16),
            ("kvb_bf", (128, NL), BF16), ("expm", (128, 12 * 512), BF16),
            ("av_sb", (128, 512), F32), ("den_sb", (128, 512), F32),
            ("recip_sb", (128, 512), F32))
        if True:

            @block.gpsimd
            def _(g):
                for mk in P1_ALL:
                    g.wait_ge(mk[0].h, mk[1])
                g.dma_start(out=masks_bf[:], in_=masksd).then_inc(p2_gld.h, 16)
                g.dma_start(out=kvb_bf[:], in_=kvbd).then_inc(p2_gld.h, 16)
            p2_gld.n += 32
            CONST2 = p2_gld.n

            kqv_g = Guard(2)
            av_g = Guard(1)

            for head in range(32):
                par = head % 2

                @block.sync
                def _(s, head=head, par=par):
                    if head == 0:
                        for mk in P1_ALL:
                            s.wait_ge(mk[0].h, mk[1])
                    kqv_g.wait(s, par)
                    s.dma_start(out=k_h[:, par * NL:(par + 1) * NL],
                                in_=k_hbm[head * 128:(head + 1) * 128, :]).then_inc(p2_ld.h, 16)
                    s.dma_start(out=q_h[:, par * NQ:(par + 1) * NQ],
                                in_=q_hbm[head * 128:(head + 1) * 128, :]).then_inc(p2_ld.h, 16)
                    for part in range(4):
                        s.dma_start(
                            out=v_h[:, par * NL + part * 768:par * NL + (part + 1) * 768]
                                .rearrange("p (kt d) -> p kt d", d=128),
                            in_=v_hbm[head, part * 768:(part + 1) * 768, :]
                                .rearrange("(kt p) d -> p kt d", p=128)).then_inc(p2_ld.h, 16)
                p2_ld.n += 96
                hld_mark = p2_ld.n

                for qt in range(4):
                    emarks = []
                    for j in range(12):
                        kt = qt * 4 + j
                        bank = j % 3
                        eb = j

                        @block.tensor
                        def _(t, j=j, kt=kt, bank=bank, par=par, qt=qt,
                              hld_mark=hld_mark):
                            if j == 0 and qt == 0:
                                t.wait_ge(p2_ld.h, hld_mark)
                            if j == 0:
                                av_g.wait(t, 0)
                            pfg.wait(t, bank)
                            t.matmul(
                                pf[bank][:],
                                k_h[:, par * NL + kt * 128:par * NL + (kt + 1) * 128],
                                q_h[:, par * NQ + qt * 512:par * NQ + (qt + 1) * 512],
                                start=True, stop=True)
                            pe_flush(t, p2_pe)
                        sc_mark = p2_pe.n

                        @block.scalar
                        def _(sc, bank=bank, eb=eb, sc_mark=sc_mark):
                            sc.wait_ge(p2_pe.h, sc_mark)
                            p2_ac.inc(sc.activation(expm[:, eb * 512:(eb + 1) * 512],
                                                    pf[bank][:], AF.Exp,
                                                    bias=EXP_SHIFT, scale=EXP_SCALE))
                        exp_mark = p2_ac.n
                        pfg.set(bank, (p2_ac, exp_mark))

                        masked = (j < 4) or (j >= 8)
                        if masked:
                            mi = j if j < 4 else 4 + (j - 8)

                            @block.vector
                            def _(v, eb=eb, mi=mi, exp_mark=exp_mark,
                                  head=head, qt=qt, j=j):
                                v.wait_ge(p2_ac.h, exp_mark)
                                if head == 0 and qt == 0 and j == 0:
                                    v.wait_ge(p2_gld.h, CONST2)
                                p2_dv.inc(v.tensor_tensor(
                                    expm[:, eb * 512:(eb + 1) * 512],
                                    expm[:, eb * 512:(eb + 1) * 512],
                                    masks_bf[:, mi * 512:(mi + 1) * 512], ALU.mult))
                            emarks.append((p2_dv, p2_dv.n))
                        else:
                            emarks.append((p2_ac, exp_mark))

                    @block.tensor
                    def _(t, qt=qt, par=par, emarks=tuple(emarks), head=head):
                        for semx, val in emarks:
                            t.wait_ge(semx.h, val)
                        if head == 0 and qt == 0:
                            t.wait_ge(p2_gld.h, CONST2)
                        av_g.wait(t, 0)
                        for j in range(12):
                            kt = qt * 4 + j
                            t.matmul(pf[3][:],
                                     v_h[:, par * NL + kt * 128:par * NL + (kt + 1) * 128],
                                     expm[:, j * 512:(j + 1) * 512],
                                     start=(j == 0), stop=(j == 11))
                            t.matmul(pf[4][:],
                                     kvb_bf[:, kt * 128:(kt + 1) * 128],
                                     expm[:, j * 512:(j + 1) * 512],
                                     start=(j == 0), stop=(j == 11))
                        pe_flush(t, p2_pe)
                    av_done = p2_pe.n

                    @block.vector
                    def _(v, head=head, qt=qt, av_done=av_done):
                        v.wait_ge(p2_pe.h, av_done)
                        v.tensor_copy(av_sb[:], pf[3][:])
                        p2_dv.inc(v.tensor_copy(den_sb[:], pf[4][:]))
                        v.wait_ge(p2_dv.h, p2_dv.n)
                        p2_dv.inc(v.reciprocal(recip_sb[:], den_sb[:]))
                        v.wait_ge(p2_dv.h, p2_dv.n)
                        p2_dv.inc(v.tensor_tensor(
                            attn_sb[:, head * 2048 + qt * 512:head * 2048 + (qt + 1) * 512],
                            av_sb[:], recip_sb[:], ALU.mult))
                    av_g.set(0, p2_dv.mark())
                kqv_g.set(par, p2_dv.mark())

        _a2.free()
        P2_ALL = [(p2_pe, p2_pe.n), (p2_ac, p2_ac.n), (p2_dv, p2_dv.n)]
        P2_DVN = p2_dv.n
        for b in range(3):
            pfg.set(b, (p2_ac, p2_ac.n))
        pfg.set(3, (p2_dv, P2_DVN))
        pfg.set(4, (p2_dv, P2_DVN))

        _a3, woblk, x_sl, h1_t, p3tmp = _arena(
            nc, ("woblk", (128, 2 * 16384), BF16), ("x_sl", (128, 2 * 512), F32),
            ("h1_t", (128, 2 * 512), F32), ("p3tmp", (128, 2 * 512), F32))
        if True:

            wo_g = Guard(2)
            xsl_g = Guard(2)
            h1t_g = Guard(2)
            p3tmp_g = Guard(2)

            for oc in range(8):
                wpar = oc % 2

                @block.gpsimd
                def _(g, oc=oc, wpar=wpar):
                    if oc == 0:
                        for mk in P2_ALL:
                            g.wait_ge(mk[0].h, mk[1])
                    wo_g.wait(g, wpar)
                    for part in range(4):
                        g.dma_start(
                            out=woblk[:, wpar * 16384:(wpar + 1) * 16384]
                                .rearrange("p (fcx o) -> p fcx o", o=512)[:, part * 8:(part + 1) * 8, :],
                            in_=woT_d[part * 1024:(part + 1) * 1024, oc * 512:(oc + 1) * 512]
                                .rearrange("(fcx p) o -> p fcx o", p=128)).then_inc(p3_gld.h, 16)
                p3_gld.n += 64
                wld_mark = p3_gld.n

                mm_last = None
                for t128 in range(16):
                    u = oc * 16 + t128
                    bank = u % 4
                    par = u % 2

                    @block.sync
                    def _(s, oc=oc, t128=t128, par=par, u=u):
                        if u == 0:
                            for mk in P2_ALL:
                                s.wait_ge(mk[0].h, mk[1])
                        xsl_g.wait(s, par)
                        s.dma_start(out=x_sl[:, par * 512:(par + 1) * 512],
                                    in_=x_own[t128 * 128:(t128 + 1) * 128, oc * 512:(oc + 1) * 512]) \
                            .then_inc(p3_ld.h, 16)
                    p3_ld.n += 16
                    x_mark = p3_ld.n

                    @block.tensor
                    def _(t, t128=t128, bank=bank, wpar=wpar, wld_mark=wld_mark):
                        if t128 == 0:
                            t.wait_ge(p3_gld.h, wld_mark)
                        pfg.wait(t, bank)
                        for fc in range(32):
                            t.matmul(
                                pf[bank][:],
                                attn_sb[:, fc * 2048 + t128 * 128:fc * 2048 + (t128 + 1) * 128],
                                woblk[:, wpar * 16384 + fc * 512:wpar * 16384 + (fc + 1) * 512],
                                start=(fc == 0), stop=(fc == 31))
                        pe_flush(t, p3_pe)
                    mm_mark = p3_pe.n
                    mm_last = mm_mark

                    @block.scalar
                    def _(sc, bank=bank, par=par, mm_mark=mm_mark):
                        sc.wait_ge(p3_pe.h, mm_mark)
                        p3tmp_gq = p3tmp_g
                        p3tmp_gq.wait(sc, par)
                        p3_ac2.inc(sc.activation(p3tmp[:, par * 512:(par + 1) * 512],
                                                 pf[bank][:], AF.Copy))
                    cp_mark = p3_ac2.n

                    @block.vector
                    def _(v, bank=bank, par=par, cp_mark=cp_mark, x_mark=x_mark):
                        v.wait_ge(p3_ac2.h, cp_mark)
                        v.wait_ge(p3_ld.h, x_mark)
                        h1t_g.wait(v, par)
                        p3_dv.inc(v.tensor_tensor(h1_t[:, par * 512:(par + 1) * 512],
                                                  p3tmp[:, par * 512:(par + 1) * 512],
                                                  x_sl[:, par * 512:(par + 1) * 512],
                                                  ALU.add))
                    h1_mark = p3_dv.n
                    p3tmp_g.set(par, (p3_dv, h1_mark))
                    pfg.set(bank, (p3_ac2, cp_mark))
                    xsl_g.set(par, (p3_dv, h1_mark))

                    @block.scalar
                    def _(s, oc=oc, t128=t128, par=par, h1_mark=h1_mark):
                        s.wait_ge(p3_dv.h, h1_mark)
                        s.dma_start(out=h1_hbm[t128 * 128:(t128 + 1) * 128, oc * 512:(oc + 1) * 512],
                                    in_=h1_t[:, par * 512:(par + 1) * 512]).then_inc(p3_st.h, 16)
                    p3_st.n += 16
                    h1t_g.set(par, (p3_st, p3_st.n))
                wo_g.set(wpar, (p3_pe, mm_last))

    _a3.free()
    _a23.free()
    P3_ST = p3_st.n
    P3_ALL = [(p3_pe, p3_pe.n), (p3_dv, p3_dv.n), (p3_st, P3_ST)]

    p4_ld = mksem("p4_ld")
    p4_wld = mksem("p4_wld")
    p4_wd = mksem("p4_wd")
    p4_dv = mksem("p4_dv")
    p4_ac = mksem("p4_ac")
    p4_pe = mksem("p4_pe")
    p4_st = mksem("p4_st")
    p4_ost = mksem("p4_ost")

    _a4, szgu_t, szd_sb, h_q, s_x, s_hn, shstage, sxT, sxTf = _arena(
        nc, ("szgu_t", (32, 512), BF16),
        ("szd_sb", (NGI, D), BF16), ("h_q", (128, NGI * 512), BF16),
        ("s_x", (32, 512), BF16),
        ("s_hn", (NGI, 512), BF16), ("shstage", (1, 512), BF16),
        ("sxT", (128, 2 * 32), BF16), ("sxTf", (128, 2 * 32), F32))
    if True:

        @block.sync
        def _(s):
            for mk in P3_ALL:
                s.wait_ge(mk[0].h, mk[1])
            s.wait_ge(p0_stA.h, SZ_DONE)
            s.dma_start(out=szd_sb[:], in_=szd_hbm).then_inc(p4_ld.h, 16)
        p4_ld.n += 16
        SZL = p4_ld.n

        st4_g = Guard(2)
        x2tm4_g = Guard(2)
        sxT_g = Guard(2)
        wgu_g = Guard(2)
        gsil_g = Guard(2)
        usb_g = Guard(2)
        dtmp_g = Guard(2)
        wdsl_g = Guard(2)
        h1sl_g = Guard(2)
        outt_g = Guard(2)
        h1t4_g = Guard(1)
        xln4_g = Guard(1)
        hq_g = Guard(1)
        sx_g = Guard(1)
        shn_g = Guard(1)
        quarter_tail = []

        for qtr in range(4):
            _a4x, x2_ln = _arena(nc, (f"x2_ln{qtr}", (128, 32 * 512), BF16))
            _a4l, h1_t4, stats4, x2tm4 = _arena(
                nc, (f"h1_t4{qtr}", (128, D), F32), (f"stats4{qtr}", (128, 24), F32),
                (f"x2tm4{qtr}", (128, 2 * D), BF16))
            if True:
                for tt in range(4):
                    par = tt % 2
                    sc0 = par * 12
                    u = qtr * 4 + tt

                    @block.sync
                    def _(s, qtr=qtr, tt=tt, u=u):
                        if tt == 0:
                            for mk in (P3_ALL if qtr == 0 else quarter_tail):
                                s.wait_ge(mk[0].h, mk[1])
                        h1t4_g.wait(s, 0)
                        row = qtr * 512 + tt * 128
                        s.dma_start(out=h1_t4[:], in_=h1_hbm[row:row + 128, :]) \
                            .then_inc(p4_ld.h, 16)
                    p4_ld.n += 16
                    h1_mark = p4_ld.n

                    @block.scalar
                    def _(sc, par=par, sc0=sc0, h1_mark=h1_mark):
                        sc.wait_ge(p4_ld.h, h1_mark)
                        st4_g.wait(sc, par)
                        x2tm4_g.wait(sc, par)
                        p4_ac.inc(sc.activation(x2tm4[:, par * D:(par + 1) * D], h1_t4[:],
                                                AF.Square, accum_out=stats4[:, sc0 + 8:sc0 + 9]))
                        sc.wait_ge(p4_ac.h, p4_ac.n)
                        p4_ac.inc(sc.activation(stats4[:, sc0:sc0 + 1],
                                                stats4[:, sc0 + 8:sc0 + 9],
                                                AF.Copy, scale=np.float32(1.0 / D)))
                    sq_mark = p4_ac.n

                    @block.vector
                    def _(v, par=par, sc0=sc0, h1_mark=h1_mark, sq_mark=sq_mark):
                        v.wait_ge(p4_ld.h, h1_mark)
                        p4_dv.inc(v.reduce_sum(stats4[:, sc0 + 9:sc0 + 10], h1_t4[:], AX.X))
                        v.wait_ge(p4_dv.h, p4_dv.n)
                        p4_dv.inc(v.tensor_scalar(stats4[:, sc0 + 1:sc0 + 2],
                                                  stats4[:, sc0 + 9:sc0 + 10],
                                                  -1.0 / D, None, ALU.mult))
                        v.wait_ge(p4_dv.h, p4_dv.n)
                        v.wait_ge(p4_ac.h, sq_mark)
                        p4_dv.inc(v.tensor_tensor(stats4[:, sc0 + 2:sc0 + 3],
                                                  stats4[:, sc0 + 1:sc0 + 2],
                                                  stats4[:, sc0 + 1:sc0 + 2], ALU.mult))
                        v.wait_ge(p4_dv.h, p4_dv.n)
                        p4_dv.inc(v.scalar_tensor_tensor(
                            stats4[:, sc0 + 2:sc0 + 3], stats4[:, sc0:sc0 + 1], 1.0,
                            stats4[:, sc0 + 2:sc0 + 3], ALU.mult, ALU.subtract))
                    var_mark = p4_dv.n

                    @block.scalar
                    def _(sc, sc0=sc0, var_mark=var_mark):
                        sc.wait_ge(p4_dv.h, var_mark)
                        p4_ac.inc(sc.activation(stats4[:, sc0 + 10:sc0 + 11],
                                                stats4[:, sc0 + 2:sc0 + 3],
                                                AF.Ln, bias=np.float32(1e-5)))
                        sc.wait_ge(p4_ac.h, p4_ac.n)
                        p4_ac.inc(sc.activation(stats4[:, sc0 + 3:sc0 + 4],
                                                stats4[:, sc0 + 10:sc0 + 11],
                                                AF.Exp, scale=np.float32(-0.5)))
                    rstd_mark = p4_ac.n

                    @block.vector
                    def _(v, qtr=qtr, tt=tt, par=par, sc0=sc0, rstd_mark=rstd_mark):
                        v.wait_ge(p4_ac.h, rstd_mark)
                        if tt == 0:
                            xln4_g.wait(v, 0)
                        p4_dv.inc(v.tensor_scalar(
                            x2tm4[:, par * D:(par + 1) * D], h1_t4[:],
                            stats4[:, sc0 + 1:sc0 + 2], stats4[:, sc0 + 3:sc0 + 4],
                            ALU.add, ALU.mult))
                    apply_mark = p4_dv.n
                    h1t4_g.set(0, (p4_dv, apply_mark))
                    st4_g.set(par, (p4_dv, apply_mark))

                    tp_last = None
                    for grp in range(4):
                        bank = grp % 2

                        @block.tensor
                        def _(t, grp=grp, bank=bank, par=par, apply_mark=apply_mark):
                            t.wait_ge(p4_dv.h, apply_mark)
                            pbg.wait(t, bank)
                            for j in range(8):
                                fc = grp * 8 + j
                                t.transpose(pbf[bank][:, j * 128:(j + 1) * 128],
                                            x2tm4[:, par * D + fc * 128:par * D + (fc + 1) * 128],
                                            ident_sb[:])
                            pe_flush(t, p4_pe)
                        tp_mark = p4_pe.n
                        tp_last = tp_mark

                        @block.vector
                        def _(v, tt=tt, grp=grp, bank=bank, tp_mark=tp_mark):
                            v.wait_ge(p4_pe.h, tp_mark)
                            dst = x2_ln[:].rearrange("p (c t) -> p c t", t=512)[
                                :, grp * 8:(grp + 1) * 8, tt * 128:(tt + 1) * 128]
                            p4_dv.inc(v.tensor_copy(
                                dst, pbf[bank][:].rearrange("p (c t) -> p c t", t=128)))
                        pbg.set(bank, p4_dv.mark())

                    @block.vector
                    def _(v, par=par, apply_mark=apply_mark):
                        v.wait_ge(p4_dv.h, apply_mark)
                        sxT_g.wait(v, par)
                        last = None
                        for fc in range(32):
                            last = v.tensor_reduce(
                                sxTf[:, par * 32 + fc:par * 32 + fc + 1],
                                x2tm4[:, par * D + fc * 128:par * D + (fc + 1) * 128],
                                AX.X, ALU.add)
                        p4_dv.inc(last)
                        v.wait_ge(p4_dv.h, p4_dv.n)
                        p4_dv.inc(v.tensor_copy(sxT[:, par * 32:(par + 1) * 32],
                                                sxTf[:, par * 32:(par + 1) * 32]))
                    sxT_mark = p4_dv.n
                    x2tm4_g.set(par, (p4_pe, tp_last), (p4_dv, sxT_mark))

                    @block.tensor
                    def _(t, par=par, tt=tt, sxT_mark=sxT_mark):
                        t.wait_ge(p4_dv.h, sxT_mark)
                        pbg.wait(t, 0)
                        if tt == 0:
                            sx_g.wait(t, 0)
                        t.transpose(pbf[0][0:32, 0:128],
                                    sxT[:, par * 32:par * 32 + 32], ident_sb[:])
                        pe_flush(t, p4_pe)
                    sxtp_mark = p4_pe.n
                    sxT_g.set(par, (p4_pe, sxtp_mark))

                    @block.vector
                    def _(v, tt=tt, sxtp_mark=sxtp_mark):
                        v.wait_ge(p4_pe.h, sxtp_mark)
                        p4_dv.inc(v.tensor_copy(s_x[0:32, tt * 128:(tt + 1) * 128],
                                                pbf[0][0:32, 0:128]))
                    pbg.set(0, p4_dv.mark())
                LN_Q = p4_dv.n
                LNQ_ALL = [(p4_dv, p4_dv.n), (p4_ac, p4_ac.n), (p4_pe, p4_pe.n)]
            _a4l.free()

            _a4g, wgu, gsil, u_sb = _arena(
                nc, (f"wgu{qtr}", (128, 4 * D), BF16), (f"gsil{qtr}", (128, 2 * 512), BF16),
                (f"u_sb{qtr}", (128, 2 * 512), F32))
            if True:

                shst_last = None
                for ic in range(NGI):
                    wpar = ic % 2
                    gb = 2 * (ic % 2)
                    ub = gb + 1

                    @block.sync
                    def _(s, qtr=qtr, ic=ic, wpar=wpar):
                        if ic == 0:
                            for mk in LNQ_ALL:
                                s.wait_ge(mk[0].h, mk[1])
                            if qtr == 0:
                                s.wait_ge(p0_stA.h, P0_STA)
                                s.wait_ge(p0_stB.h, P0_STB)
                        wgu_g.wait(s, wpar)
                        s.dma_start(out=wgu[:, wpar * 2 * D:wpar * 2 * D + D],
                                    in_=wg_hbm[ic, :, :]).then_inc(p4_wld.h, 16)
                        s.dma_start(out=wgu[:, wpar * 2 * D + D:wpar * 2 * D + 2 * D],
                                    in_=wu_hbm[ic, :, :]).then_inc(p4_wld.h, 16)
                        s.dma_start(out=szgu_t[0:32, wpar * 256:wpar * 256 + 128],
                                    in_=szg_hbm[0:32, ic * 128:(ic + 1) * 128]).then_inc(p4_wld.h, 16)
                        s.dma_start(out=szgu_t[0:32, wpar * 256 + 128:wpar * 256 + 256],
                                    in_=szu_hbm[0:32, ic * 128:(ic + 1) * 128]).then_inc(p4_wld.h, 16)
                    p4_wld.n += 64
                    wgu_mark = p4_wld.n

                    @block.tensor
                    def _(t, qtr=qtr, ic=ic, wpar=wpar, gb=gb, ub=ub, wgu_mark=wgu_mark):
                        t.wait_ge(p4_wld.h, wgu_mark)
                        if ic == 0:
                            t.wait_ge(p4_dv.h, LN_Q)
                            t.wait_ge(p4_ld.h, SZL)
                        pfg.wait(t, gb)
                        pfg.wait(t, ub)
                        for fc in range(32):
                            t.matmul(pf[gb][:],
                                     wgu[:, wpar * 2 * D + fc * 128:wpar * 2 * D + (fc + 1) * 128],
                                     x2_ln[:, fc * 512:(fc + 1) * 512],
                                     start=(fc == 0), stop=False)
                        t.matmul(pf[gb][:], szgu_t[0:32, wpar * 256:wpar * 256 + 128],
                                 s_x[:], start=False, stop=True)
                        for fc in range(32):
                            t.matmul(pf[ub][:],
                                     wgu[:, wpar * 2 * D + D + fc * 128:wpar * 2 * D + D + (fc + 1) * 128],
                                     x2_ln[:, fc * 512:(fc + 1) * 512],
                                     start=(fc == 0), stop=False)
                        t.matmul(pf[ub][:],
                                 szgu_t[0:32, wpar * 256 + 128:wpar * 256 + 256],
                                 s_x[:], start=False, stop=True)
                        pe_flush(t, p4_pe)
                    gu_mark = p4_pe.n
                    wgu_g.set(wpar, (p4_pe, gu_mark))

                    @block.scalar
                    def _(sc, ic=ic, gb=gb, gu_mark=gu_mark):
                        sc.wait_ge(p4_pe.h, gu_mark)
                        gsil_g.wait(sc, ic % 2)
                        p4_ac.inc(sc.activation(gsil[:, (ic % 2) * 512:(ic % 2 + 1) * 512],
                                                pf[gb][:], AF.Silu))
                    sil_mark = p4_ac.n
                    pfg.set(gb, (p4_ac, sil_mark))

                    @block.vector
                    def _(v, qtr=qtr, ic=ic, ub=ub, sil_mark=sil_mark):
                        v.wait_ge(p4_ac.h, sil_mark)
                        if ic == 0:
                            hq_g.wait(v, 0)
                        p4_dv.inc(v.tensor_tensor(h_q[:, ic * 512:(ic + 1) * 512],
                                                  gsil[:, (ic % 2) * 512:(ic % 2 + 1) * 512],
                                                  pf[ub][:], ALU.mult))
                    hm_mark = p4_dv.n
                    pfg.set(ub, (p4_dv, hm_mark))
                    gsil_g.set(ic % 2, (p4_dv, hm_mark))

                    @block.tensor
                    def _(t, qtr=qtr, ic=ic, hm_mark=hm_mark):
                        t.wait_ge(p4_dv.h, hm_mark)
                        pfg.wait(t, 4)
                        t.matmul(pf[4][0:1, :], negones[:],
                                 h_q[:, ic * 512:(ic + 1) * 512],
                                 start=True, stop=True)
                        pe_flush(t, p4_pe)
                    sh_mark = p4_pe.n

                    @block.vector
                    def _(v, sh_mark=sh_mark):
                        v.wait_ge(p4_pe.h, sh_mark)
                        shn_g.wait(v, 0)
                        p4_dv.inc(v.tensor_copy(shstage[:], pf[4][0:1, :]))
                    shev_mark = p4_dv.n
                    pfg.set(4, (p4_dv, shev_mark))

                    @block.sync
                    def _(s, ic=ic, shev_mark=shev_mark):
                        s.wait_ge(p4_dv.h, shev_mark)
                        s.dma_start(out=s_hn[ic:ic + 1, :], in_=shstage[:]).then_inc(p4_st.h, 16)
                    p4_st.n += 16
                    shn_g.set(0, (p4_st, p4_st.n))
                    shst_last = p4_st.n
                GU_Q = p4_pe.n
                xln4_g.set(0, (p4_pe, GU_Q))
                sx_g.set(0, (p4_pe, GU_Q))
            if p4_dbg is not None:
                @block.sync
                def _(s, qtr=qtr):
                    s.wait_ge(p4_pe.h, GU_Q)
                    s.wait_ge(p4_dv.h, p4_dv.n)
                    s.wait_ge(p4_st.h, p4_st.n)
                    s.dma_start(out=p4_dbg["hq"][qtr], in_=h_q[:]).then_inc(p4_st.h, 16)
                    s.dma_start(out=p4_dbg["sx"][qtr], in_=s_x[:]).then_inc(p4_st.h, 16)
                    s.dma_start(out=p4_dbg["shn"][qtr], in_=s_hn[:]).then_inc(p4_st.h, 16)
                    s.dma_start(out=p4_dbg["x2"][qtr], in_=x2_ln[:]).then_inc(p4_st.h, 16)
                p4_st.n += 64
                GU_DUMP = [(p4_st, p4_st.n)]
            else:
                GU_DUMP = []
            _a4g.free()
            _a4x.free()
            GU_TAIL = [(p4_pe, p4_pe.n), (p4_dv, p4_dv.n), (p4_ac, p4_ac.n)] + GU_DUMP
            _a4d, wdsl, h1sl, out_t, dtmp = _arena(
                nc, (f"wdsl{qtr}", (128, 2 * NGI * 256), BF16),
                (f"h1sl{qtr}", (128, 2 * 256), F32),
                (f"out_t{qtr}", (128, 2 * 256), F32),
                (f"dtmp{qtr}", (128, 2 * 256), F32))
            if True:

                for osub in range(16):
                    wpar = osub % 2

                    @block.sync
                    def _(s, qtr=qtr, osub=osub, wpar=wpar, shst_last=shst_last):
                        if osub == 0:
                            for mk in GU_TAIL:
                                s.wait_ge(mk[0].h, mk[1])
                        wdsl_g.wait(s, wpar)
                        s.dma_start(
                            out=wdsl[:, wpar * NGI * 256:(wpar + 1) * NGI * 256],
                            in_=wd_hbm[osub].rearrange("p i o -> p (i o)")) \
                            .then_inc(p4_wd.h, 16)
                    p4_wd.n += 16
                    wd_mark = p4_wd.n

                    @block.tensor
                    def _(t, qtr=qtr, osub=osub, wpar=wpar, wd_mark=wd_mark,
                          shst_last=shst_last):
                        t.wait_ge(p4_wd.h, wd_mark)
                        if osub == 0:
                            t.wait_ge(p4_st.h, shst_last)
                        for b in range(4):
                            pfg.wait(t, b)
                        for tt in range(4):
                            t.matmul(pf[tt][:, 0:256], s_hn[:, tt * 128:(tt + 1) * 128],
                                     szd_sb[:, osub * 256:(osub + 1) * 256],
                                     start=True, stop=False)
                        for ic in range(NGI):
                            for tt in range(4):
                                t.matmul(pf[tt][:, 0:256],
                                         h_q[:, ic * 512 + tt * 128:ic * 512 + (tt + 1) * 128],
                                         wdsl[:, (wpar * NGI + ic) * 256:(wpar * NGI + ic + 1) * 256],
                                         start=False, stop=(ic == NGI - 1))
                        pe_flush(t, p4_pe)
                    dcor_mark = p4_pe.n
                    wdsl_g.set(wpar, (p4_pe, dcor_mark))

                    for tt in range(4):
                        par = tt % 2

                        @block.sync
                        def _(s, qtr=qtr, osub=osub, tt=tt, par=par):
                            h1sl_g.wait(s, par)
                            row = qtr * 512 + tt * 128
                            s.dma_start(out=h1sl[:, par * 256:(par + 1) * 256],
                                        in_=h1_hbm[row:row + 128, osub * 256:(osub + 1) * 256]) \
                                .then_inc(p4_ld.h, 16)
                        p4_ld.n += 16
                        h1sl_mark = p4_ld.n

                        @block.scalar
                        def _(sc, tt=tt, par=par, dcor_mark=dcor_mark):
                            sc.wait_ge(p4_pe.h, dcor_mark)
                            dtmp_g.wait(sc, par)
                            p4_ac.inc(sc.activation(dtmp[:, par * 256:(par + 1) * 256],
                                                    pf[tt][:, 0:256], AF.Copy))
                        cp_mark = p4_ac.n
                        pfg.set(tt, (p4_ac, cp_mark))

                        @block.vector
                        def _(v, tt=tt, par=par, cp_mark=cp_mark, h1sl_mark=h1sl_mark):
                            v.wait_ge(p4_ac.h, cp_mark)
                            v.wait_ge(p4_ld.h, h1sl_mark)
                            outt_g.wait(v, par)
                            p4_dv.inc(v.tensor_tensor(
                                out_t[:, par * 256:(par + 1) * 256],
                                dtmp[:, par * 256:(par + 1) * 256],
                                h1sl[:, par * 256:(par + 1) * 256], ALU.add))
                        oev_mark = p4_dv.n
                        h1sl_g.set(par, (p4_dv, oev_mark))
                        dtmp_g.set(par, (p4_dv, oev_mark))

                        @block.scalar
                        def _(s, qtr=qtr, osub=osub, tt=tt, par=par, oev_mark=oev_mark):
                            s.wait_ge(p4_dv.h, oev_mark)
                            row = qtr * 512 + tt * 128
                            s.dma_start(out=out_d[row:row + 128, osub * 256:(osub + 1) * 256],
                                        in_=out_t[:, par * 256:(par + 1) * 256]) \
                                .then_inc(p4_ost.h, 16)
                        p4_ost.n += 16
                        outt_g.set(par, (p4_ost, p4_ost.n))
            _a4d.free()
            hq_g.set(0, (p4_pe, p4_pe.n))
            quarter_tail = [(p4_pe, p4_pe.n), (p4_dv, p4_dv.n), (p4_ac, p4_ac.n),
                            (p4_ost, p4_ost.n), (p4_st, p4_st.n)]

    _a4.free()
    FINAL = p4_ost.n

    @block.sync
    def _(s):
        s.wait_ge(p4_ost.h, FINAL)

    for cm in reversed(ctxs):
        cm.__exit__(None, None, None)
    return nc


_NC_CACHE = None


def kernel(**inputs):
    global _NC_CACHE
    in_maps = host_prep(inputs)
    if _NC_CACHE is None:
        _NC_CACHE = build_nc()
    res = run_bass_kernel_spmd(_NC_CACHE, in_maps, core_ids=list(range(8)))
    out = np.empty((4, 4096, 4096), np.float32)
    for c in range(8):
        b, h = c // 2, c % 2
        out[b, h * 2048:(h + 1) * 2048] = res.results[c]["out"]
    return out
